# version 01
# baseline (speedup 1.0000x reference)
import numpy as np
from contextlib import ExitStack

import concourse.bass as bass
import concourse.mybir as mybir
from concourse.bass_utils import run_bass_kernel_spmd

F32 = mybir.dt.float32
BF16 = mybir.dt.bfloat16
U8 = mybir.dt.uint8
AF = mybir.ActivationFunctionType
ALU = mybir.AluOpType
AX = mybir.AxisListType

ENGS = ("tensor", "vector", "scalar", "gpsimd", "sync")


class Cfg:
    def __init__(self, NB=4, S=2048, D=1024, DQ=256, E=8, F=4096, FG=4, TBW=512):
        self.NB, self.S, self.D, self.DQ, self.E, self.F = NB, S, D, DQ, E, F
        self.KD = D // 128
        self.KQ = DQ // 128
        self.NT = S // 128
        self.NFL = F // 128
        self.FG = FG
        self.NFG = self.NFL // FG
        self.TBW = TBW
        self.NTB = S // TBW
        self.TPB = TBW // 128
        self.eps_ln = 1e-5
        self.eps_rms = float(np.finfo(np.float32).eps)
        self.scale = 1.0 / float(np.sqrt(DQ))
        self.debug = False
        self.only_a = False
        self.only_ln = False


class Sem:
    def __init__(self, h):
        self.h = h
        self.n = 0

    def inc(self, amt=1):
        self.n += amt
        return (self.h, amt)


class Q:
    def __init__(self):
        self.ops = {e: [] for e in ENGS}

    def op(self, eng, fn, waits=(), inc=None):
        self.ops[eng].append((fn, [x for x in waits if x is not None], inc))

    def run_block(self, nc):
        with nc.Block() as block:
            for eng in ENGS:
                ops = self.ops[eng]
                if not ops:
                    continue

                def make(ops):
                    def prog(e):
                        for fn, waits, inc in ops:
                            for (sem, val) in waits:
                                e.wait_ge(sem, val)
                            ins = fn(e)
                            if inc is not None:
                                ins.then_inc(inc[0], inc[1])
                    return prog

                getattr(block, eng)(make(ops))
        self.ops = {e: [] for e in ENGS}


def build_program(cfg: Cfg):
    nc = bass.Bass("TRN2", target_bir_lowering=False)
    NB, S, D, DQ, E, F = cfg.NB, cfg.S, cfg.D, cfg.DQ, cfg.E, cfg.F
    KD, KQ, NT, NFL, FG, NFG = cfg.KD, cfg.KQ, cfg.NT, cfg.NFL, cfg.FG, cfg.NFG
    TBW, NTB, TPB = cfg.TBW, cfg.NTB, cfg.TPB

    x_h = nc.dram_tensor("x", [NB * S, D], F32, kind="ExternalInput").ap()
    wq_h = nc.dram_tensor("wq", [D, DQ], F32, kind="ExternalInput").ap()
    wk_h = nc.dram_tensor("wk", [D, DQ], F32, kind="ExternalInput").ap()
    wvd_h = nc.dram_tensor("wvd", [D, DQ], F32, kind="ExternalInput").ap()
    wvu_h = nc.dram_tensor("wvu", [DQ, D], F32, kind="ExternalInput").ap()
    lnw_h = nc.dram_tensor("lnw", [D], F32, kind="ExternalInput").ap()
    wg_h = nc.dram_tensor("wg", [D, E], F32, kind="ExternalInput").ap()
    bg_h = nc.dram_tensor("bg", [E], F32, kind="ExternalInput").ap()
    w1a_h = nc.dram_tensor("w1a", [E, D, F], F32, kind="ExternalInput").ap()
    b1a_h = nc.dram_tensor("b1a", [128, E * NFL], F32, kind="ExternalInput").ap()
    w1b_h = nc.dram_tensor("w1b", [E, D, F], F32, kind="ExternalInput").ap()
    b1b_h = nc.dram_tensor("b1b", [128, E * NFL], F32, kind="ExternalInput").ap()
    w2_h = nc.dram_tensor("w2", [E, F, D], F32, kind="ExternalInput").ap()
    b2_h = nc.dram_tensor("b2", [E, D], F32, kind="ExternalInput").ap()
    ident_h = nc.dram_tensor("ident", [128, 128], F32, kind="ExternalInput").ap()
    amask_h = nc.dram_tensor("amask", [128, 128], U8, kind="ExternalInput").ap()
    o_h = nc.dram_tensor("o", [NB * S, D], F32, kind="ExternalOutput").ap()
    dbgk = "ExternalOutput" if cfg.debug else "Internal"
    xn2s_h = nc.dram_tensor("xn2s", [KD, 128, NB * S], BF16, kind=dbgk).ap()
    s_h = nc.dram_tensor("sscr", [NB * S, D], F32, kind=dbgk).ap()
    if cfg.debug:
        weo_h = nc.dram_tensor("weo", [128, NB * NT * E], F32, kind="ExternalOutput").ap()
        go_h = nc.dram_tensor("go", [NB * S, D], F32, kind="ExternalOutput").ap()
        ao_h = nc.dram_tensor("ao", [NB * S, S], F32, kind="ExternalOutput").ap()
        xnso_h = nc.dram_tensor("xnso", [KD, 128, NB * S], BF16, kind="ExternalOutput").ap()
        colso_h = nc.dram_tensor("colso", [NB * NT, 128, 8], F32, kind="ExternalOutput").ap()
        lgo_h = nc.dram_tensor("lgo", [NB, 128, NT * E], F32, kind="ExternalOutput").ap()
        vno_h = nc.dram_tensor("vno", [NB, 128, S // 128 * D], BF16, kind="ExternalOutput").ap()
        qo_h = nc.dram_tensor("qo", [NB, 128, (DQ // 128) * S], BF16, kind="ExternalOutput").ap()
        ko_h = nc.dram_tensor("ko", [NB, 128, (DQ // 128) * S], BF16, kind="ExternalOutput").ap()

    st = ExitStack()

    def sbuf(name, shape, dt):
        return st.enter_context(nc.sbuf_tensor(name, shape, dt))

    def psum(name, shape, dt=F32):
        return st.enter_context(nc.psum_tensor(name, shape, dt))

    def sem(name):
        return Sem(st.enter_context(nc.semaphore(name)))

    we_all = sbuf("we_all", [128, NB * NT * E], F32)
    weT_all = sbuf("weT_all", [E, NB * S], BF16)
    b1a_sb = sbuf("b1a_sb", [128, E * NFL], F32)
    b1b_sb = sbuf("b1b_sb", [128, E * NFL], F32)
    b2_sb = sbuf("b2_sb", [E, D], BF16)
    ident_sb = sbuf("ident_sb", [128, 128], F32)
    amask_sb = sbuf("amask_sb", [128, 128], U8)
    neg_sb = sbuf("neg_sb", [128, 128], F32)
    cols = sbuf("colsA", [128, 32], F32)
    epsln_c = sbuf("epsln_c", [128, 1], F32)
    epsrms_c = sbuf("epsrms_c", [128, 1], F32)

    ps_big = psum("ps_big", [128, 2048])
    ps_o0 = psum("ps_o0", [128, 1024])
    ps_o1 = psum("ps_o1", [128, 1024])

    s_pe = sem("s_pe")
    s_dv = sem("s_dv")
    s_ac = sem("s_ac")
    s_wc = sem("s_wc")
    s_x = [sem("s_x0"), sem("s_x1")]
    s_se = [sem("s_se0"), sem("s_se1")]
    s_spx = sem("s_spx")
    s_w1a = [sem("s_w1a0"), sem("s_w1a1")]
    s_w1b = [sem("s_w1b0"), sem("s_w1b1")]
    s_w2 = [sem("s_w20"), sem("s_w21")]
    s_xn2r = sem("s_xn2r")
    s_xc = [sem("s_xc0"), sem("s_xc1")]
    s_sc = [sem("s_sc0"), sem("s_sc1")]
    s_out = [sem("s_out0"), sem("s_out1")]
    s_dbg = sem("s_dbg")
    s_wqkv = sem("s_wqkv")

    q = Q()

    def w(s, val):
        return (s.h, val) if val > 0 else None

    psg = {r: {"dv": 0, "ac": 0} for r in
           ("bigA", "ba0", "bb0", "ba1", "bb1", "o0h0", "o0h1", "o1h0", "o1h1")}

    def pwait(*regions):
        rs = set(regions)
        quarters = {"ba0", "bb0", "ba1", "bb1"}
        if "bigA" in rs:
            rs |= quarters
        elif rs & quarters:
            rs.add("bigA")
        dv = max(psg[r]["dv"] for r in rs)
        ac = max(psg[r]["ac"] for r in rs)
        return [w(s_dv, dv), w(s_ac, ac)]

    def pread(region, kind, cnt):
        psg[region][kind] = max(psg[region][kind], cnt)

    stA = ExitStack()

    def sbufA(name, shape, dt):
        return stA.enter_context(nc.sbuf_tensor(name, shape, dt))

    assert KD * DQ == KQ * D
    wqkv_sb = sbufA("wqkv_sb", [128, KD * DQ], BF16)
    wg_sb = sbufA("wg_sb", [128, KD * E], BF16)
    lnw_bc = sbufA("lnw_bc", [128, D], F32)
    bg_bc = sbufA("bg_bc", [128, E], F32)
    xin = [sbufA("xin0", [128, D], F32), sbufA("xin1", [128, D], F32)]
    sqs = sbufA("sqs", [128, D], BF16)
    xnf = [sbufA("xnf0", [128, D], F32), sbufA("xnf1", [128, D], F32)]
    xn2f = [sbufA("xn2f0", [128, D], F32), sbufA("xn2f1", [128, D], F32)]
    xnT = sbufA("xnT", [128, KD * S], BF16)
    xn2T = sbufA("xn2T_A", [128, KD * S], BF16)
    logits = sbufA("logits", [128, NT * E], F32)
    glt = sbufA("glt", [128, 4 * NT], F32)
    geq = sbufA("geq", [128, 2 * NT * E], F32)
    qT_sb = sbufA("qT", [128, KQ * S], BF16)
    kT_sb = sbufA("kT", [128, KQ * S], BF16)
    vdT_sb = sbufA("vdT", [128, KQ * S], BF16)
    vnat = sbufA("vnat", [128, NT * D], BF16)
    expb = sbufA("expb", [128, S], F32)
    attnT = sbufA("attnT", [128, S], BF16)
    sev = [sbufA("sev0", [128, D], F32), sbufA("sev1", [128, D], F32)]

    def ld3(dst, src, kk):
        return lambda g: g.dma_start(
            out=dst[:].rearrange("p (k m) -> p k m", k=kk),
            in_=src.rearrange("(k p) m -> p k m", p=128))

    q.op("gpsimd", ld3(wg_sb, wg_h, KD), inc=s_wc.inc(16))
    q.op("gpsimd", lambda g: g.dma_start(out=lnw_bc[:], in_=lnw_h[None, :].broadcast_to([128, D])),
         inc=s_wc.inc(16))
    q.op("gpsimd", lambda g: g.dma_start(out=bg_bc[:], in_=bg_h[None, :].broadcast_to([128, E])),
         inc=s_wc.inc(16))
    q.op("gpsimd", lambda g: g.dma_start(out=b1a_sb[:], in_=b1a_h[:]), inc=s_wc.inc(16))
    q.op("gpsimd", lambda g: g.dma_start(out=b1b_sb[:], in_=b1b_h[:]), inc=s_wc.inc(16))
    q.op("gpsimd", lambda g: g.dma_start(out=b2_sb[:], in_=b2_h[:]), inc=s_wc.inc(16))
    q.op("gpsimd", lambda g: g.dma_start(out=ident_sb[:], in_=ident_h[:]), inc=s_wc.inc(16))
    q.op("gpsimd", lambda g: g.dma_start(out=amask_sb[:], in_=amask_h[:]), inc=s_wc.inc(16))
    q.op("gpsimd", lambda g: g.memset(neg_sb[:], -1e30), inc=s_wc.inc(1))
    q.op("gpsimd", lambda g: g.memset(epsln_c[:], float(cfg.eps_ln)), inc=s_wc.inc(1))
    q.op("gpsimd", lambda g: g.memset(epsrms_c[:], float(cfg.eps_rms)), inc=s_wc.inc(1))
    WC = s_wc.n
    wcw = (s_wc.h, WC)

    dbg_cols = [0, 0]
    dbg_xnt = 0
    x_free = [0, 0]
    xnf_free = [0, 0]
    xn2f_free = [0, 0]
    sev_rec = [0, 0]
    we_done = [0] * NB
    smm_last = [0] * NB
    wqkv_carry = [0]

    for b in range(NB):
        logit_guard = 0

        for tt in range(NT):
            sl = tt % 2
            par = tt % 2
            row0 = b * S + tt * 128
            cb = 16 * par
            q.op("sync", (lambda r, sl: lambda e: e.dma_start(
                out=xin[sl][:], in_=x_h[r:r + 128, :]))(r=row0, sl=sl),
                waits=[w(s_dv, x_free[sl])], inc=s_x[sl].inc(16))
            xw = [(s_x[sl].h, s_x[sl].n)]
            q.op("scalar", (lambda sl, cb: lambda e: e.activation(
                sqs[:], xin[sl][:], AF.Square, accum_out=cols[:, cb + 1:cb + 2]))(sl, cb),
                waits=xw + [w(s_dv, x_free[sl])] +
                ([w(s_dbg, dbg_cols[par])] if cfg.debug else []), inc=s_ac.inc())
            ac_sq = s_ac.n
            q.op("vector", (lambda sl, cb: lambda e: e.reduce_sum(
                cols[:, cb:cb + 1], xin[sl][:], axis=AX.X))(sl, cb),
                waits=xw + ([w(s_dbg, dbg_cols[par])] if cfg.debug else []),
                inc=s_dv.inc())
            dv_rs = s_dv.n
            q.op("vector", (lambda cb: lambda e: e.tensor_scalar_mul(
                cols[:, cb + 2:cb + 3], cols[:, cb:cb + 1], 1.0 / D))(cb),
                waits=[w(s_dv, dv_rs)], inc=s_dv.inc())
            dv_mean = s_dv.n
            q.op("vector", (lambda cb: lambda e: e.tensor_scalar_mul(
                cols[:, cb + 5:cb + 6], cols[:, cb + 1:cb + 2], 1.0 / D))(cb),
                waits=[w(s_ac, ac_sq)], inc=s_dv.inc())
            dv_ex2 = s_dv.n
            q.op("vector", (lambda cb: lambda e: e.scalar_tensor_tensor(
                cols[:, cb + 6:cb + 7], cols[:, cb + 2:cb + 3], cols[:, cb + 2:cb + 3],
                cols[:, cb + 5:cb + 6], ALU.mult, ALU.subtract))(cb),
                waits=[w(s_dv, dv_mean), w(s_dv, dv_ex2)], inc=s_dv.inc())
            dv_nv = s_dv.n
            q.op("scalar", (lambda cb: lambda e: e.activation(
                cols[:, cb + 3:cb + 4], cols[:, cb + 6:cb + 7], AF.Sqrt,
                scale=-1.0, bias=epsln_c[:]))(cb),
                waits=[w(s_dv, dv_nv), wcw], inc=s_ac.inc())
            ac_sd = s_ac.n
            q.op("vector", (lambda cb: lambda e: e.reciprocal(
                cols[:, cb + 3:cb + 4], cols[:, cb + 3:cb + 4]))(cb),
                waits=[w(s_ac, ac_sd)], inc=s_dv.inc())
            dv_rstd = s_dv.n
            q.op("vector", (lambda sl, par, cb: lambda e: e.tensor_scalar(
                xnf[par][:], xin[sl][:], cols[:, cb + 2:cb + 3], cols[:, cb + 3:cb + 4],
                ALU.subtract, ALU.mult))(sl, par, cb),
                waits=[w(s_pe, xnf_free[par]), w(s_dv, dv_rstd)], inc=s_dv.inc())
            dv_xn = s_dv.n
            x_free[sl] = dv_xn
            q.op("scalar", (lambda par, cb: lambda e: e.activation(
                sqs[:], xnf[par][:], AF.Square, accum_out=cols[:, cb + 7:cb + 8]))(par, cb),
                waits=[w(s_dv, dv_xn)], inc=s_ac.inc())
            ac_s2 = s_ac.n
            q.op("scalar", (lambda cb: lambda e: e.activation(
                cols[:, cb + 4:cb + 5], cols[:, cb + 7:cb + 8], AF.Sqrt,
                scale=1.0 / D, bias=epsrms_c[:]))(cb),
                waits=[w(s_ac, ac_s2)], inc=s_ac.inc())
            ac_sd2 = s_ac.n
            q.op("vector", (lambda cb: lambda e: e.reciprocal(
                cols[:, cb + 4:cb + 5], cols[:, cb + 4:cb + 5]))(cb),
                waits=[w(s_ac, ac_sd2)], inc=s_dv.inc())
            dv_rstd2 = s_dv.n
            q.op("vector", (lambda par, cb: lambda e: e.scalar_tensor_tensor(
                xn2f[par][:], xnf[par][:], cols[:, cb + 4:cb + 5], lnw_bc[:],
                ALU.mult, ALU.mult))(par, cb),
                waits=[wcw, w(s_pe, xn2f_free[par]), w(s_dv, dv_rstd2)], inc=s_dv.inc())
            dv_xn2 = s_dv.n

            if cfg.debug:
                q.op("sync", (lambda b, tt, cb: lambda e: e.dma_start(
                    out=colso_h[b * NT + tt], in_=cols[:, cb:cb + 8]))(b, tt, cb),
                    waits=[w(s_dv, dv_xn2)], inc=s_dbg.inc(16))
                dbg_cols[par] = s_dbg.n

            for which, srcf, dstT, dvready in ((0, xnf[par], xnT, dv_xn),
                                               (1, xn2f[par], xn2T, dv_xn2)):
                for c0 in range(0, KD, 4):
                    half = (c0 // 4) % 2
                    reg = "o0h%d" % half
                    n = min(4, KD - c0)
                    for j in range(n):
                        q.op("tensor", (lambda srcf, c0, j, half: lambda e: e.transpose(
                            ps_o0[:, half * 512 + j * 128: half * 512 + (j + 1) * 128],
                            srcf[:, (c0 + j) * 128: (c0 + j + 1) * 128], ident_sb[:]))(
                                srcf, c0, j, half),
                            waits=([w(s_dv, dvready), wcw] + pwait(reg) +
                                   ([w(s_dbg, dbg_xnt)] if cfg.debug else [])) if j == 0 else [],
                            inc=s_pe.inc() if j == n - 1 else None)
                    pe_tr = s_pe.n
                    extra = []
                    if which == 1 and c0 == 0:
                        extra = [w(s_spx, 16 * KD * b)]
                    q.op("vector", (lambda dstT, c0, n, half, tt: lambda e: e.tensor_copy(
                        dstT[:].rearrange("p (k s) -> p k s", k=KD)[
                            :, c0:c0 + n, tt * 128:(tt + 1) * 128],
                        ps_o0[:, half * 512: half * 512 + n * 128].rearrange(
                            "p (k s) -> p k s", k=n)))(dstT, c0, n, half, tt),
                        waits=[w(s_pe, pe_tr)] + extra, inc=s_dv.inc())
                    pread(reg, "dv", s_dv.n)
                if which == 0:
                    xnf_free[par] = pe_tr
                    xnT_done = s_dv.n
                else:
                    xn2f_free[par] = pe_tr
                    xn2T_done = s_dv.n

            for k in range(KD):
                q.op("tensor", (lambda k, tt: lambda e: e.matmul(
                    ps_o1[:, 0:E],
                    xn2T[:, k * S + tt * 128: k * S + (tt + 1) * 128],
                    wg_sb[:, k * E:(k + 1) * E],
                    start=(k == 0), stop=(k == KD - 1)))(k, tt),
                    waits=([w(s_dv, xn2T_done)] + pwait("o1h0")) if k == 0 else [],
                    inc=s_pe.inc() if k == KD - 1 else None)
            pe_lg = s_pe.n
            q.op("vector", (lambda tt: lambda e: e.tensor_add(
                logits[:, tt * E:(tt + 1) * E], ps_o1[:, 0:E], bg_bc[:]))(tt),
                waits=[w(s_pe, pe_lg), wcw], inc=s_dv.inc())
            pread("o1h0", "dv", s_dv.n)
            lg_done = s_dv.n

        if cfg.debug:
            q.op("sync", (lambda b: lambda e: e.dma_start(
                out=lgo_h[b], in_=logits[:]))(b),
                waits=[w(s_dv, s_dv.n)], inc=s_dbg.inc(16))
            dbg_lg = s_dbg.n
        lg3 = logits[:].rearrange("p (t e) -> p t e", e=E)
        m1 = glt[:, 0 * NT:1 * NT]
        m2 = glt[:, 1 * NT:2 * NT]
        w1c = glt[:, 2 * NT:3 * NT]
        w2c = glt[:, 3 * NT:4 * NT]
        eq1 = geq[:, 0:NT * E]
        eq2 = geq[:, NT * E:2 * NT * E]
        eq13 = eq1.rearrange("p (t e) -> p t e", e=E)
        eq23 = eq2.rearrange("p (t e) -> p t e", e=E)
        q.op("vector", lambda e: e.reduce_max(m1, lg3, axis=AX.X),
             waits=[w(s_dv, lg_done)] + ([w(s_dbg, dbg_lg)] if cfg.debug else []),
             inc=s_dv.inc())
        dv_m1 = s_dv.n
        q.op("vector", lambda e: e.tensor_tensor(
            eq13, lg3, m1.unsqueeze(2).broadcast_to([128, NT, E]), ALU.is_equal),
            waits=[w(s_dv, dv_m1)], inc=s_dv.inc())
        dv_eq1 = s_dv.n
        q.op("vector", lambda e: e.scalar_tensor_tensor(
            eq2, eq1, -1e30, logits[:], ALU.mult, ALU.add),
            waits=[w(s_dv, dv_eq1)], inc=s_dv.inc())
        dv_msk = s_dv.n
        q.op("vector", lambda e: e.reduce_max(m2, eq23, axis=AX.X),
             waits=[w(s_dv, dv_msk)], inc=s_dv.inc())
        dv_m2 = s_dv.n
        q.op("vector", lambda e: e.tensor_sub(w1c, m1, m2), waits=[w(s_dv, dv_m2)],
             inc=s_dv.inc())
        dv_w1 = s_dv.n
        q.op("scalar", lambda e: e.activation(w1c, w1c, AF.Sigmoid),
             waits=[w(s_dv, dv_w1)], inc=s_ac.inc())
        ac_w1 = s_ac.n
        q.op("vector", lambda e: e.tensor_tensor(
            eq23, eq23, m2.unsqueeze(2).broadcast_to([128, NT, E]), ALU.is_equal),
            waits=[w(s_ac, ac_w1)], inc=s_dv.inc())
        dv_eq2 = s_dv.n
        q.op("vector", lambda e: e.tensor_scalar(w2c, w1c, -1.0, 1.0, ALU.mult, ALU.add),
             inc=s_dv.inc())
        dv_w2 = s_dv.n
        q.op("vector", lambda e: e.tensor_tensor(
            eq13, eq13, w1c.unsqueeze(2).broadcast_to([128, NT, E]), ALU.mult),
            waits=[w(s_dv, dv_eq2)], inc=s_dv.inc())
        dv_p1 = s_dv.n
        q.op("vector", lambda e: e.tensor_tensor(
            eq23, eq23, w2c.unsqueeze(2).broadcast_to([128, NT, E]), ALU.mult),
            waits=[w(s_dv, dv_w2)], inc=s_dv.inc())
        dv_p2 = s_dv.n
        q.op("vector", (lambda b: lambda e: e.tensor_add(
            we_all[:, b * NT * E:(b + 1) * NT * E], eq1, eq2))(b),
            waits=[w(s_dv, dv_p1), w(s_dv, dv_p2)], inc=s_dv.inc())
        dv_we = s_dv.n

        for c0 in range(0, NT, 4):
            half = (c0 // 4) % 2
            reg = "o0h%d" % half
            n = min(4, NT - c0)
            for j in range(n):
                q.op("tensor", (lambda b, c0, j, half: lambda e: e.transpose(
                    ps_o0[0:E, half * 512 + j * 128: half * 512 + (j + 1) * 128],
                    we_all[:, (b * NT + c0 + j) * E:(b * NT + c0 + j + 1) * E],
                    ident_sb[:]))(b, c0, j, half),
                    waits=([w(s_dv, dv_we)] + pwait(reg)) if j == 0 else [],
                    inc=s_pe.inc() if j == n - 1 else None)
            pe_wt = s_pe.n
            q.op("vector", (lambda b, c0, n, half: lambda e: e.tensor_copy(
                weT_all[0:E, b * S + c0 * 128: b * S + (c0 + n) * 128],
                ps_o0[0:E, half * 512: half * 512 + n * 128]))(b, c0, n, half),
                waits=[w(s_pe, pe_wt)], inc=s_dv.inc())
            pread(reg, "dv", s_dv.n)
        we_done[b] = s_dv.n

        if cfg.debug:
            for k in range(KD):
                q.op("sync", (lambda b, k: lambda e: e.dma_start(
                    out=xnso_h[k, :, b * S:(b + 1) * S],
                    in_=xnT[:, k * S:(k + 1) * S]))(b, k),
                    waits=[w(s_dv, xnT_done)] if k == 0 else [], inc=s_dbg.inc(16))
            dbg_xnt = s_dbg.n
        for k in range(KD):
            q.op("sync", (lambda b, k: lambda e: e.dma_start(
                out=xn2s_h[k, :, b * S:(b + 1) * S],
                in_=xn2T[:, k * S:(k + 1) * S]))(b, k),
                waits=[w(s_dv, xn2T_done)] if k == 0 else [], inc=s_spx.inc(16))

        wqkv_last_pe = wqkv_carry[0]
        if not cfg.only_ln:
            for (wsrc, kk, dstT) in ((wq_h, KD, qT_sb), (wk_h, KD, kT_sb), (wvd_h, KD, vdT_sb)):
                q.op("gpsimd", (lambda wsrc, kk: ld3(wqkv_sb, wsrc, kk))(wsrc, kk),
                     waits=[w(s_pe, wqkv_last_pe)], inc=s_wqkv.inc(16))
                wq_n = s_wqkv.n
                for m in range(KQ):
                    for c in range(0, S, 512):
                        cw = min(512, S - c)
                        half = (c // 512) % 2
                        reg = "o1h%d" % half
                        for k in range(KD):
                            q.op("tensor", (lambda m, c, cw, k, half: lambda e: e.matmul(
                                ps_o1[:, half * 512: half * 512 + cw],
                                wqkv_sb[:, k * DQ + m * 128: k * DQ + (m + 1) * 128],
                                xnT[:, k * S + c: k * S + c + cw],
                                start=(k == 0), stop=(k == KD - 1)))(m, c, cw, k, half),
                                waits=([w(s_dv, xnT_done), (s_wqkv.h, wq_n)] + pwait(reg))
                                if k == 0 else [],
                                inc=s_pe.inc() if k == KD - 1 else None)
                        pe_g = s_pe.n
                        q.op("vector", (lambda dstT, m, c, cw, half: lambda e: e.tensor_copy(
                            dstT[:, m * S + c: m * S + c + cw],
                            ps_o1[:, half * 512: half * 512 + cw]))(dstT, m, c, cw, half),
                            waits=[w(s_pe, pe_g)], inc=s_dv.inc())
                        pread(reg, "dv", s_dv.n)
                wqkv_last_pe = s_pe.n
        qkv_done = s_dv.n

        if not cfg.only_ln:
            q.op("gpsimd", ld3(wqkv_sb, wvu_h, KQ),
                 waits=[w(s_pe, wqkv_last_pe)], inc=s_wqkv.inc(16))
            wvu_n = s_wqkv.n
        for tt in range(0 if cfg.only_ln else NT):
            regs = ("o0h0", "o0h1") if tt % 2 == 0 else ("o1h0", "o1h1")
            pst = ps_o0 if tt % 2 == 0 else ps_o1
            nmm = 0
            tot = ((D + 511) // 512) * KQ
            for h in range(0, D, 512):
                hw = min(512, D - h)
                for kq in range(KQ):
                    nmm += 1
                    q.op("tensor", (lambda tt, h, hw, kq, pst: lambda e: e.matmul(
                        pst[:, h:h + hw],
                        vdT_sb[:, kq * S + tt * 128: kq * S + (tt + 1) * 128],
                        wqkv_sb[:, kq * D + h: kq * D + h + hw],
                        start=(kq == 0), stop=(kq == KQ - 1)))(tt, h, hw, kq, pst),
                        waits=(pwait(*regs) + [w(s_dv, qkv_done), (s_wqkv.h, wvu_n)])
                        if nmm == 1 else [],
                        inc=s_pe.inc() if nmm == tot else None)
            pe_v = s_pe.n
            q.op("vector", (lambda tt, pst: lambda e: e.tensor_copy(
                vnat[:, tt * D:(tt + 1) * D], pst[:, 0:D]))(tt, pst),
                waits=[w(s_pe, pe_v)], inc=s_dv.inc())
            for r in regs:
                pread(r, "dv", s_dv.n)
        if not cfg.only_ln:
            wqkv_carry[0] = s_pe.n

        expb_free = 0

        def emit_scores(tt):
            ncols = (tt + 1) * 128
            nblk = (ncols + 511) // 512
            for cbk in range(nblk):
                cw = min(512, ncols - cbk * 512)
                for kq in range(KQ):
                    q.op("tensor", (lambda tt, cbk, cw, kq: lambda e: e.matmul(
                        ps_big[:, cbk * 512: cbk * 512 + cw],
                        qT_sb[:, kq * S + tt * 128: kq * S + (tt + 1) * 128],
                        kT_sb[:, kq * S + cbk * 512: kq * S + cbk * 512 + cw],
                        start=(kq == 0), stop=(kq == KQ - 1)))(tt, cbk, cw, kq),
                        waits=pwait("bigA") if (cbk == 0 and kq == 0) else [],
                        inc=s_pe.inc() if (cbk == nblk - 1 and kq == KQ - 1) else None)
            return s_pe.n

        def emit_softmax(tt, pe_sc):
            ncols = (tt + 1) * 128
            q.op("vector", (lambda tt: lambda e: e.copy_predicated(
                ps_big[:, tt * 128:(tt + 1) * 128], amask_sb[:], neg_sb[:]))(tt),
                waits=[w(s_pe, pe_sc), wcw], inc=s_dv.inc())
            dv_mask = s_dv.n
            pread("bigA", "dv", dv_mask)
            q.op("vector", (lambda ncols: lambda e: e.reduce_max(
                cols[:, 16:17], ps_big[:, 0:ncols], axis=AX.X, negate=True))(ncols),
                waits=[w(s_dv, dv_mask)], inc=s_dv.inc())
            dv_mx = s_dv.n
            pread("bigA", "dv", dv_mx)
            q.op("scalar", lambda e: e.activation(
                cols[:, 17:18], cols[:, 16:17], AF.Copy, scale=float(cfg.scale)),
                waits=[w(s_dv, dv_mx)], inc=s_ac.inc())
            ac_cp = s_ac.n
            q.op("scalar", (lambda ncols: lambda e: e.activation(
                expb[:, 0:ncols], ps_big[:, 0:ncols], AF.Exp,
                bias=cols[:, 17:18], scale=float(cfg.scale),
                accum_out=cols[:, 18:19]))(ncols),
                waits=[w(s_pe, expb_free), w(s_ac, ac_cp)] +
                      ([(s_dbg.h, s_dbg.n)] if cfg.debug else []), inc=s_ac.inc())
            ac_exp = s_ac.n
            pread("bigA", "ac", ac_exp)
            q.op("vector", lambda e: e.reciprocal(cols[:, 19:20], cols[:, 18:19]),
                 waits=[w(s_ac, ac_exp)], inc=s_dv.inc())
            dv_rc = s_dv.n
            q.op("vector", (lambda ncols: lambda e: e.tensor_scalar_mul(
                expb[:, 0:ncols], expb[:, 0:ncols], cols[:, 19:20]))(ncols),
                waits=[w(s_dv, dv_rc)], inc=s_dv.inc())
            return s_dv.n

        def emit_trans_smm(tt, dv_attn):
            nonlocal expb_free
            n_tk = tt + 1
            if cfg.debug:
                rr = b * S + tt * 128
                q.op("sync", (lambda rr, n_tk: lambda e: e.dma_start(
                    out=ao_h[rr:rr + 128, 0:n_tk * 128], in_=expb[:, 0:n_tk * 128]))(rr, n_tk),
                    waits=[w(s_dv, dv_attn)], inc=s_dbg.inc(16))
            for c0 in range(0, n_tk, 4):
                half = (c0 // 4) % 2
                reg = "o0h%d" % half
                n = min(4, n_tk - c0)
                for j in range(n):
                    q.op("tensor", (lambda c0, j, half: lambda e: e.transpose(
                        ps_o0[:, half * 512 + j * 128: half * 512 + (j + 1) * 128],
                        expb[:, (c0 + j) * 128: (c0 + j + 1) * 128], ident_sb[:]))(c0, j, half),
                        waits=([w(s_dv, dv_attn)] + pwait(reg)) if j == 0 else [],
                        inc=s_pe.inc() if j == n - 1 else None)
                pe_tr = s_pe.n
                q.op("vector", (lambda c0, n, half: lambda e: e.tensor_copy(
                    attnT[:, c0 * 128:(c0 + n) * 128],
                    ps_o0[:, half * 512: half * 512 + n * 128]))(c0, n, half),
                    waits=[w(s_pe, pe_tr)], inc=s_dv.inc())
                pread(reg, "dv", s_dv.n)
            expb_free = s_pe.n
            attnT_done = s_dv.n
            nmm = 0
            tot = n_tk * ((D + 511) // 512)
            for tk in range(n_tk):
                for h in range(0, D, 512):
                    hw = min(512, D - h)
                    nmm += 1
                    q.op("tensor", (lambda tk, h, hw, n_tk: lambda e: e.matmul(
                        ps_o1[:, h:h + hw],
                        attnT[:, tk * 128:(tk + 1) * 128],
                        vnat[:, tk * D + h: tk * D + h + hw],
                        start=(tk == 0), stop=(tk == n_tk - 1)))(tk, h, hw, n_tk),
                        waits=([w(s_dv, attnT_done)] + pwait("o1h0", "o1h1"))
                        if nmm == 1 else [],
                        inc=s_pe.inc() if nmm == tot else None)
            pe_s = s_pe.n
            sl = tt % 2
            q.op("vector", (lambda sl: lambda e: e.tensor_copy(
                sev[sl][:], ps_o1[:, 0:D]))(sl),
                waits=[w(s_pe, pe_s), (s_se[sl].h, sev_rec[sl]) if sev_rec[sl] else None],
                inc=s_dv.inc())
            pread("o1h0", "dv", s_dv.n)
            pread("o1h1", "dv", s_dv.n)
            dv_sev = s_dv.n
            row0 = b * S + tt * 128
            q.op("sync", (lambda sl, row0: lambda e: e.dma_start(
                out=s_h[row0:row0 + 128, :], in_=sev[sl][:]))(sl, row0),
                waits=[w(s_dv, dv_sev)], inc=s_se[sl].inc(16))
            sev_rec[sl] = s_se[sl].n
            return pe_s

        prev = None
        for tt in range(0 if cfg.only_ln else NT):
            pe_sc = emit_scores(tt)
            if prev is not None:
                smm_last[b] = emit_trans_smm(prev[0], prev[1])
            dv_attn = emit_softmax(tt, pe_sc)
            prev = (tt, dv_attn)
        if prev is not None:
            smm_last[b] = emit_trans_smm(prev[0], prev[1])
        if cfg.debug:
            vnat_dv = s_dv.n
            q.op("sync", (lambda b: lambda e: e.dma_start(
                out=vno_h[b], in_=vnat[:]))(b), waits=[w(s_dv, vnat_dv)], inc=s_dbg.inc(16))
            q.op("sync", (lambda b: lambda e: e.dma_start(
                out=qo_h[b], in_=qT_sb[:]))(b), inc=s_dbg.inc(16))
            q.op("sync", (lambda b: lambda e: e.dma_start(
                out=ko_h[b], in_=kT_sb[:]))(b), inc=s_dbg.inc(16))

    PE_END_A = s_pe.n
    DV_END_A = s_dv.n
    AC_END_A = s_ac.n
    SE_END = (s_se[0].n, s_se[1].n)

    q.run_block(nc)
    stA.close()

    stB = ExitStack()

    def sbufB(name, shape, dt):
        return stB.enter_context(nc.sbuf_tensor(name, shape, dt))

    xn2B = sbufB("xn2B", [128, KD * S], BF16)
    acc = sbufB("acc", [128, NT * D], F32)
    w1a_b = [sbufB("w1a_b0", [128, KD * FG * 128], BF16),
             sbufB("w1a_b1", [128, KD * FG * 128], BF16)]
    w1b_b = [sbufB("w1b_b0", [128, KD * FG * 128], BF16),
             sbufB("w1b_b1", [128, KD * FG * 128], BF16)]
    w2_b = [sbufB("w2_b0", [128, FG * D], BF16),
            sbufB("w2_b1", [128, FG * D], BF16)]
    yT = [sbufB("yT0", [128, FG * TBW], BF16), sbufB("yT1", [128, FG * TBW], BF16)]
    silu_sb = sbufB("silu_sb", [128, TBW], BF16)
    xcb = [sbufB("xc0", [128, D], F32), sbufB("xc1", [128, D], F32)]
    scb = [sbufB("scb0", [128, D], F32), sbufB("scb1", [128, D], F32)]
    gsb = sbufB("gsb", [128, D], F32)
    osb = [sbufB("osb0", [128, D], F32), sbufB("osb1", [128, D], F32)]

    barrier = [(s_pe.h, PE_END_A), (s_dv.h, DV_END_A), (s_ac.h, AC_END_A),
               (s_spx.h, 16 * KD * NB),
               (s_se[0].h, SE_END[0]) if SE_END[0] else None,
               (s_se[1].h, SE_END[1]) if SE_END[1] else None]
    barrier = [x for x in barrier if x is not None]

    state = dict(w1_free=[0, 0], w2_free=[0, 0], yT_free=[0, 0], yT_ready=[0, 0],
                 silu_free=0, fl_last_pe=0, c_done=0, g_free=0,
                 xc_rec=[0, 0], sc_rec=[0, 0], osb_rec=[0, 0],
                 acc_ready=[0] * NT, first=True)

    def emit_w_loads(ts, e_i, fg):
        gidx = (ts * E + e_i) * NFG + fg
        slot = gidx % 2
        extra = list(barrier) if state["first"] else []
        state["first"] = False
        q.op("gpsimd", (lambda e_i, fg, slot: lambda e: e.dma_start(
            out=w1a_b[slot][:].rearrange("p (k f) -> p k f", k=KD),
            in_=w1a_h[e_i, :, fg * FG * 128:(fg + 1) * FG * 128].rearrange(
                "(k p) f -> p k f", p=128)))(e_i, fg, slot),
            waits=extra + [w(s_pe, state["w1_free"][slot])], inc=s_w1a[slot].inc(16))
        q.op("gpsimd", (lambda e_i, fg, slot: lambda e: e.dma_start(
            out=w1b_b[slot][:].rearrange("p (k f) -> p k f", k=KD),
            in_=w1b_h[e_i, :, fg * FG * 128:(fg + 1) * FG * 128].rearrange(
                "(k p) f -> p k f", p=128)))(e_i, fg, slot),
            inc=s_w1b[slot].inc(16))
        q.op("gpsimd", (lambda e_i, fg, slot: lambda e: e.dma_start(
            out=w2_b[slot][:].rearrange("p (f d) -> p f d", f=FG),
            in_=w2_h[e_i, fg * FG * 128:(fg + 1) * FG * 128, :].rearrange(
                "(f p) d -> p f d", p=128)))(e_i, fg, slot),
            waits=[w(s_pe, state["w2_free"][slot])], inc=s_w2[slot].inc(16))
        return slot, s_w1a[slot].n, s_w1b[slot].n, s_w2[slot].n

    def emit_fl(ts, e_i, fg, tb, wslot, w1a_n, w1b_n, xn2_n, unit):
        ys = unit % 2
        for fl in range(FG):
            par = fl % 2
            ra, rb = "ba%d" % par, "bb%d" % par
            pa = ps_big[:, par * 1024: par * 1024 + TBW]
            pb = ps_big[:, par * 1024 + 512: par * 1024 + 512 + TBW]
            for k in range(KD):
                waits = []
                if k == 0:
                    waits = pwait(ra) + [(s_w1a[wslot].h, w1a_n), (s_xn2r.h, xn2_n)]
                q.op("tensor", (lambda fl, k, tb, wslot, pa: lambda e: e.matmul(
                    pa, w1a_b[wslot][:, k * FG * 128 + fl * 128: k * FG * 128 + (fl + 1) * 128],
                    xn2B[:, k * S + tb * TBW: k * S + (tb + 1) * TBW],
                    start=(k == 0), stop=(k == KD - 1)))(fl, k, tb, wslot, pa),
                    waits=waits, inc=s_pe.inc() if k == KD - 1 else None)
            pe_a = s_pe.n
            for k in range(KD):
                waits = []
                if k == 0:
                    waits = pwait(rb) + [(s_w1b[wslot].h, w1b_n)]
                q.op("tensor", (lambda fl, k, tb, wslot, pb: lambda e: e.matmul(
                    pb, w1b_b[wslot][:, k * FG * 128 + fl * 128: k * FG * 128 + (fl + 1) * 128],
                    xn2B[:, k * S + tb * TBW: k * S + (tb + 1) * TBW],
                    start=(k == 0), stop=(k == KD - 1)))(fl, k, tb, wslot, pb),
                    waits=waits, inc=s_pe.inc() if k == KD - 1 else None)
            pe_b = s_pe.n
            bcol = e_i * NFL + fg * FG + fl
            q.op("scalar", (lambda bcol, pa: lambda e: e.activation(
                silu_sb[:], pa, AF.Silu, bias=b1a_sb[:, bcol:bcol + 1]))(bcol, pa),
                waits=[w(s_pe, pe_a), w(s_dv, state["silu_free"]), wcw], inc=s_ac.inc())
            pread(ra, "ac", s_ac.n)
            ac_silu = s_ac.n
            q.op("vector", (lambda bcol, pb, ys, fl: lambda e: e.scalar_tensor_tensor(
                yT[ys][:, fl * TBW:(fl + 1) * TBW], pb, b1b_sb[:, bcol:bcol + 1],
                silu_sb[:], ALU.add, ALU.mult))(bcol, pb, ys, fl),
                waits=[w(s_pe, pe_b), w(s_ac, ac_silu), wcw] +
                      ([w(s_pe, state["yT_free"][ys])] if fl == 0 else []),
                inc=s_dv.inc())
            pread(rb, "dv", s_dv.n)
            state["silu_free"] = s_dv.n
        state["yT_ready"][ys] = s_dv.n
        state["fl_last_pe"] = s_pe.n
        state["w1_free"][wslot] = s_pe.n

    def emit_o(ts, e_i, fg, tb, wslot, w2_n, unit):
        ys = unit % 2
        for ttl in range(TPB):
            tt = tb * TPB + ttl
            po = ps_o0 if ttl % 2 == 0 else ps_o1
            regs = ("o0h0", "o0h1") if ttl % 2 == 0 else ("o1h0", "o1h1")
            nmm = 0
            tot = FG * ((D + 511) // 512)
            for fl in range(FG):
                for h in range(0, D, 512):
                    hw = min(512, D - h)
                    nmm += 1
                    waits = []
                    if nmm == 1:
                        waits = pwait(*regs) + [w(s_dv, state["yT_ready"][ys]),
                                                (s_w2[wslot].h, w2_n)]
                    q.op("tensor", (lambda fl, h, hw, ttl, ys, wslot, po: lambda e: e.matmul(
                        po[:, h:h + hw],
                        yT[ys][:, fl * TBW + ttl * 128: fl * TBW + (ttl + 1) * 128],
                        w2_b[wslot][:, fl * D + h: fl * D + h + hw],
                        start=(fl == 0), stop=(fl == FG - 1)))(fl, h, hw, ttl, ys, wslot, po),
                        waits=waits,
                        inc=s_pe.inc() if nmm == tot else None)
            pe_o = s_pe.n
            state["w2_free"][wslot] = pe_o
            state["yT_free"][ys] = pe_o
            q.op("vector", (lambda ts, e_i, tt, po: lambda e: e.scalar_tensor_tensor(
                acc[:, tt * D:(tt + 1) * D], po[:, 0:D],
                we_all[:, (ts * NT + tt) * E + e_i:(ts * NT + tt) * E + e_i + 1],
                acc[:, tt * D:(tt + 1) * D], ALU.mult, ALU.add))(ts, e_i, tt, po),
                waits=[w(s_pe, pe_o), w(s_dv, state["acc_ready"][tt])], inc=s_dv.inc())
            for r in regs:
                pread(r, "dv", s_dv.n)
            state["acc_ready"][tt] = s_dv.n

    def emit_phase_c(ts):
        for tt in range(NT):
            sl = tt % 2
            row0 = ts * S + tt * 128
            q.op("sync", (lambda row0, sl: lambda e: e.dma_start(
                out=xcb[sl][:], in_=x_h[row0:row0 + 128, :]))(row0, sl),
                waits=[w(s_dv, state["xc_rec"][sl])] +
                      (list(barrier) if (ts == 0 and tt < 2) else []),
                inc=s_xc[sl].inc(16))
            xn_ = s_xc[sl].n
            q.op("sync", (lambda row0, sl: lambda e: e.dma_start(
                out=scb[sl][:], in_=s_h[row0:row0 + 128, :]))(row0, sl),
                waits=[w(s_dv, state["sc_rec"][sl])], inc=s_sc[sl].inc(16))
            sn_ = s_sc[sl].n
            q.op("scalar", (lambda tt: lambda e: e.activation(
                gsb[:], acc[:, tt * D:(tt + 1) * D], AF.Silu))(tt),
                waits=[w(s_dv, state["acc_ready"][tt]), w(s_dv, state["g_free"])],
                inc=s_ac.inc())
            ac_g = s_ac.n
            gw = []
            if cfg.debug:
                q.op("sync", (lambda row0: lambda e: e.dma_start(
                    out=go_h[row0:row0 + 128, :], in_=gsb[:]))(row0),
                    waits=[w(s_ac, ac_g)], inc=s_dbg.inc(16))
                gw = [(s_dbg.h, s_dbg.n)]
            q.op("vector", (lambda sl: lambda e: e.tensor_mul(gsb[:], gsb[:], scb[sl][:]))(sl),
                 waits=[w(s_ac, ac_g), (s_sc[sl].h, sn_)] + gw, inc=s_dv.inc())
            dv_gm = s_dv.n
            q.op("vector", (lambda sl: lambda e: e.tensor_add(
                osb[sl][:], gsb[:], xcb[sl][:]))(sl),
                waits=[(s_xc[sl].h, xn_), w(s_dv, dv_gm),
                       (s_out[sl].h, state["osb_rec"][sl]) if state["osb_rec"][sl] else None],
                inc=s_dv.inc())
            dv_o = s_dv.n
            state["sc_rec"][sl] = dv_o
            state["xc_rec"][sl] = dv_o
            state["g_free"] = dv_o
            q.op("sync", (lambda row0, sl: lambda e: e.dma_start(
                out=o_h[row0:row0 + 128, :], in_=osb[sl][:]))(row0, sl),
                waits=[w(s_dv, dv_o)], inc=s_out[sl].inc(16))
            state["osb_rec"][sl] = s_out[sl].n
        state["c_done"] = s_dv.n

    for ts in range(NB if not cfg.only_a else 0):
        if ts == 0:
            waits0 = list(barrier)
        else:
            waits0 = [w(s_pe, state["fl_last_pe"])]
        for k in range(KD):
            q.op("sync", (lambda ts, k: lambda e: e.dma_start(
                out=xn2B[:, k * S:(k + 1) * S],
                in_=xn2s_h[k, :, ts * S:(ts + 1) * S]))(ts, k),
                waits=waits0 if k == 0 else [], inc=s_xn2r.inc(16))
        xn2_n = s_xn2r.n

        for tt in range(NT):
            po = ps_o0 if tt % 2 == 0 else ps_o1
            regs = ("o0h0", "o0h1") if tt % 2 == 0 else ("o1h0", "o1h1")
            nh = (D + 511) // 512
            for hi, h in enumerate(range(0, D, 512)):
                hw = min(512, D - h)
                waits = []
                if hi == 0:
                    waits = pwait(*regs) + [w(s_dv, we_done[ts]), wcw]
                    if ts == 0 and tt == 0:
                        waits = waits + list(barrier)
                q.op("tensor", (lambda ts, tt, h, hw, po: lambda e: e.matmul(
                    po[:, h:h + hw],
                    weT_all[0:E, ts * S + tt * 128: ts * S + (tt + 1) * 128],
                    b2_sb[0:E, h:h + hw],
                    start=True, stop=True))(ts, tt, h, hw, po),
                    waits=waits, inc=s_pe.inc() if hi == nh - 1 else None)
            pe_b2 = s_pe.n
            q.op("vector", (lambda tt, po: lambda e: e.tensor_copy(
                acc[:, tt * D:(tt + 1) * D], po[:, 0:D]))(tt, po),
                waits=[w(s_pe, pe_b2), w(s_dv, state["c_done"])], inc=s_dv.inc())
            for r in regs:
                pread(r, "dv", s_dv.n)
            state["acc_ready"][tt] = s_dv.n

        pending = None
        for e_i in range(E):
            for fg in range(NFG):
                wslot, w1a_n, w1b_n, w2_n = emit_w_loads(ts, e_i, fg)
                for tb in range(NTB):
                    unit = (e_i * NFG + fg) * NTB + tb
                    emit_fl(ts, e_i, fg, tb, wslot, w1a_n, w1b_n, xn2_n, unit)
                    if pending is not None:
                        emit_o(*pending)
                    pending = (ts, e_i, fg, tb, wslot, w2_n, unit)
        emit_o(*pending)

        emit_phase_c(ts)

    if cfg.debug:
        q.op("sync", lambda e: e.dma_start(out=weo_h[:], in_=we_all[:]),
             waits=[(s_dv.h, s_dv.n)], inc=s_dbg.inc(16))
        q.op("sync", lambda e: e.wait_ge(s_dbg.h, s_dbg.n))

    if not cfg.only_a:
        q.op("sync", lambda e: e.wait_ge(s_out[0].h, s_out[0].n),
             waits=[(s_out[1].h, s_out[1].n)])
    else:
        q.op("sync", lambda e: e.wait_ge(s_spx.h, s_spx.n),
             waits=[(s_se[0].h, s_se[0].n), (s_se[1].h, s_se[1].n),
                    (s_dv.h, s_dv.n), (s_ac.h, s_ac.n), (s_pe.h, s_pe.n)] +
                   ([(s_dbg.h, s_dbg.n)] if cfg.debug else []))

    q.run_block(nc)
    stB.close()
    st.close()
    return nc


_CFG = Cfg()
_NC_CACHE = {}


def _get_nc(cfg):
    key = id(cfg)
    if key not in _NC_CACHE:
        _NC_CACHE[key] = build_program(cfg)
    return _NC_CACHE[key]


def make_in_maps(cfg, n_cores, x, wq, wk, wvd, wvu, lnw, wg, bg, w1a, b1a, w1b, b1b, w2, b2):
    B, S, D = x.shape
    per = B // n_cores
    ident = np.eye(128, dtype=np.float32)
    amask = np.triu(np.ones((128, 128), dtype=np.uint8), 1)
    f32 = lambda a: np.ascontiguousarray(np.asarray(a, dtype=np.float32))

    def btr(bv):
        bv = f32(bv)
        E_, F_ = bv.shape
        return np.ascontiguousarray(
            bv.reshape(E_, F_ // 128, 128).transpose(2, 0, 1).reshape(128, E_ * (F_ // 128)))

    shared = dict(wq=f32(wq), wk=f32(wk), wvd=f32(wvd), wvu=f32(wvu), lnw=f32(lnw),
                  wg=f32(wg), bg=f32(bg), w1a=f32(w1a), b1a=btr(b1a), w1b=f32(w1b),
                  b1b=btr(b1b), w2=f32(w2), b2=f32(b2), ident=ident, amask=amask)
    in_maps = []
    for c in range(n_cores):
        m = dict(shared)
        m["x"] = np.ascontiguousarray(x[c * per:(c + 1) * per].reshape(per * S, D))
        in_maps.append(m)
    return in_maps


def kernel(L0_attn_inputs, L0_attn_Wq, L0_attn_Wk, L0_attn_Wvdown, L0_attn_Wvup,
           L0_ffn_ln_w, L0_ffn_Wg, L0_ffn_bg, L0_ffn_W1a, L0_ffn_b1a,
           L0_ffn_W1b, L0_ffn_b1b, L0_ffn_W2, L0_ffn_b2):
    cfg = _CFG
    n_cores = 8
    x = np.ascontiguousarray(np.asarray(L0_attn_inputs, dtype=np.float32))
    B, S, D = x.shape
    per = B // n_cores
    nc = _get_nc(cfg)
    in_maps = make_in_maps(cfg, n_cores, x, L0_attn_Wq, L0_attn_Wk, L0_attn_Wvdown,
                           L0_attn_Wvup, L0_ffn_ln_w, L0_ffn_Wg, L0_ffn_bg,
                           L0_ffn_W1a, L0_ffn_b1a, L0_ffn_W1b, L0_ffn_b1b,
                           L0_ffn_W2, L0_ffn_b2)
    res = run_bass_kernel_spmd(nc, in_maps, core_ids=list(range(n_cores)))
    out = np.concatenate([np.asarray(r["o"]).reshape(per, S, D) for r in res.results], axis=0)
    return out.astype(np.float32)
